# version 02
# baseline (speedup 1.0000x reference)
import numpy as np
from contextlib import ExitStack
import concourse.bass as bass
import concourse.mybir as mybir
from concourse.bass_utils import run_bass_kernel_spmd

F32 = mybir.dt.float32
BF16 = mybir.dt.bfloat16
FP16 = mybir.dt.float16
AF = mybir.ActivationFunctionType
ALU = mybir.AluOpType
AX = mybir.AxisListType

T = 2048
E = 1024
DH = 64
NH = 16
NLOC = 4
NHASH = 4
S = 8192
CS = 64
NCHUNK = 128
NTILE = 16
FC = 8
MVAL = -3.4e38
TSV = -50000.0
LNE = 1e-5
EPSN = 1e-20

ENGS = ("pe", "dve", "act", "sync", "gp")
SEM_OF = {"pe": "pe", "dve": "dve", "act": "act", "sync": "sd", "gp": "gd"}


class Sched:

    def __init__(self):
        self.ops = {e: [] for e in ENGS}
        self.c = {s: 0 for s in ("pe", "dve", "act", "sd", "gd", "od")}

    def emit(self, eng, fn, waits=(), inc=None, inc_amt=1):
        w = []
        for item in waits:
            if isinstance(item, str):
                w.append((item, self.c[item]))
            else:
                w.append(item)
        tgt = None
        if inc is not None:
            self.c[inc] += inc_amt
            tgt = (inc, self.c[inc])
        self.ops[eng].append((w, fn, tgt))
        return tgt

    def now(self, sem):
        return self.c[sem]


def build_nc(stages=99, klsh=99, kdump=""):
    nc = bass.Bass("TRN2", target_bir_lowering=False)
    dt = nc.dram_tensor

    xin = dt("xin", [2, T, E], F32, kind="ExternalInput").ap()
    w_qk = dt("w_qk", [8, 128, 1024], F32, kind="ExternalInput").ap()
    w_v = dt("w_v", [8, 128, 1024], F32, kind="ExternalInput").ap()
    w_out = dt("w_out", [8, 128, 1024], F32, kind="ExternalInput").ap()
    w_ff1 = dt("w_ff1", [32, 128, 1024], F32, kind="ExternalInput").ap()
    w_ff2 = dt("w_ff2", [4096, 1024], F32, kind="ExternalInput").ap()
    w_r = dt("w_r", [8, 128, 1024], F32, kind="ExternalInput").ap()
    w_k = dt("w_k", [8, 128, 1024], F32, kind="ExternalInput").ap()
    w_v2 = dt("w_v2", [8, 128, 1024], F32, kind="ExternalInput").ap()
    rotc = dt("rotc", [DH, 128], F32, kind="ExternalInput").ap()
    pcols = dt("pcols", [128, 64], F32, kind="ExternalInput").ap()
    bff1c = dt("bff1c", [128, 32], F32, kind="ExternalInput").ap()
    idn_bf = dt("idn_bf", [128, 128], BF16, kind="ExternalInput").ap()
    idn_f16 = dt("idn_f16", [128, 128], FP16, kind="ExternalInput").ap()
    idn_f32 = dt("idn_f32", [128, 128], F32, kind="ExternalInput").ap()
    supper_bf = dt("supper_bf", [128, 128], BF16, kind="ExternalInput").ap()
    slt_b5 = dt("slt_b5", [32, 800], F32, kind="ExternalInput").ap()
    iota64r = dt("iota64r", [128, 64], BF16, kind="ExternalInput").ap()
    iota128r = dt("iota128r", [128, 128], BF16, kind="ExternalInput").ap()
    iota2048r = dt("iota2048r", [128, 2048], FP16, kind="ExternalInput").ap()
    iotacol = dt("iotacol", [128, 1], F32, kind="ExternalInput").ap()
    thresh64 = dt("thresh64", [128, 64], BF16, kind="ExternalInput").ap()
    ones_f32 = dt("ones_f32", [128, 128], F32, kind="ExternalInput").ap()
    ones_bf = dt("ones_bf", [128, 128], BF16, kind="ExternalInput").ap()
    ones_f16 = dt("ones_f16", [128, 128], FP16, kind="ExternalInput").ap()
    mask_w = dt("mask_w", [128, 256], F32, kind="ExternalInput").ap()
    constc = dt("constc", [128, 4], F32, kind="ExternalInput").ap()
    ju = dt("ju", [128, 256], BF16, kind="ExternalInput").ap()
    mask_half = dt("mask_half", [128, 192], F32, kind="ExternalInput").ap()
    ival = dt("ival", [128, 16], F32, kind="ExternalInput").ap()
    ival16 = dt("ival16", [128, 16], FP16, kind="ExternalInput").ap()
    dscr = dt("dscr", [32768], F32)
    xout = dt("xout", [2, T, E], F32, kind="ExternalOutput").ap()
    dbg = dt("dbg", [128, 8192], F32, kind="ExternalOutput").ap()

    sch = Sched()

    with ExitStack() as stack:
        en = stack.enter_context

        def sb(name, shp, d=BF16):
            return en(nc.sbuf_tensor(name, shp, d))

        Z0 = sb("Z0", [128, NTILE * E])
        AO = sb("AO", [128, NTILE * E])
        A1 = sb("A1", [128, NTILE * E])
        A2 = sb("A2", [128, NTILE * E])
        A3 = sb("A3", [128, NTILE * E])
        WS = [sb("WS%d" % i, [128, 1024]) for i in range(4)]
        QKG = sb("QKG", [128, T])
        VG = sb("VG", [128, T])
        QH = sb("QH", [64, T])
        ATN = sb("ATN", [128, 1024])
        PT = sb("PT", [128, 256])
        s_idn = sb("s_idn", [128, 128])
        s_idn16 = sb("s_idn16", [128, 128], FP16)
        s_idn32 = sb("s_idn32", [128, 128], F32)
        s_sup = sb("s_sup", [128, 128])
        s_sltb5 = sb("s_sltb5", [32, 800], F32)
        s_i64 = sb("s_i64", [128, 64])
        s_i128 = sb("s_i128", [128, 128])
        s_i2048 = sb("s_i2048", [128, 2048], FP16)
        s_icol = sb("s_icol", [128, 1], F32)
        s_th64 = sb("s_th64", [128, 64])
        s_ones32 = sb("s_ones32", [128, 128], F32)
        s_onesbf = sb("s_onesbf", [128, 128])
        s_ones16 = sb("s_ones16", [128, 128], FP16)
        s_maskw = sb("s_maskw", [128, 256], F32)
        s_rot = sb("s_rot", [DH, 128])
        s_pc = sb("s_pc", [128, 64], F32)
        s_bff1 = sb("s_bff1", [128, 32], F32)
        s_cc = sb("s_cc", [128, 4], F32)
        s_ju = sb("s_ju", [128, 256])
        s_maskh = sb("s_maskh", [128, 192], F32)
        s_ival = sb("s_ival", [128, 16], F32)
        s_ival16 = sb("s_ival16", [128, 16], FP16)
        st = sb("st", [128, 64], F32)
        t32a = sb("t32a", [128, 1024], F32)
        t32b = sb("t32b", [128, 512], F32)
        pb = [en(nc.psum_tensor("pb%d" % i, [128, 512], F32)) for i in range(4)]
        pc = [en(nc.psum_tensor("pc%d" % i, [128, 1024], F32)) for i in range(2)]

        def gbank(i):
            if i < 4:
                return pb[i].ap()
            j, h = divmod(i - 4, 2)
            return pc[j].ap()[:, h * 512:(h + 1) * 512]
        pbank = gbank

        def pbank16(i):
            if i < 4:
                return pb[i].ap().bitcast(BF16)
            j, h = divmod(i - 4, 2)
            return pc[j].ap().bitcast(BF16)[:, h * 1024:(h + 1) * 1024]

        def pbankh(i):
            if i < 4:
                return pb[i].ap().bitcast(FP16)
            j, h = divmod(i - 4, 2)
            return pc[j].ap().bitcast(FP16)[:, h * 1024:(h + 1) * 1024]

        def pcb(i):
            return pc[i].ap()

        sems = {}
        for s in ("pe", "dve", "act", "sd", "gd", "od"):
            sems[s] = en(nc.semaphore("sem_" + s))
        block = en(nc.Block())

        def W(eng, wlist):
            for sname, val in wlist:
                if val > 0:
                    eng.wait_ge(sems[sname], val)

        def dma(queue, out, in_, waits=()):
            semn = "sd" if queue == "sync" else "gd"
            def fn(e, w, out=out, in_=in_, semn=semn):
                W(e, w)
                e.dma_start(out=out, in_=in_).then_inc(sems[semn], 16)
            return sch.emit(queue, fn, waits=waits, inc=semn, inc_amt=16)

        class _Px:
            def __init__(self, e):
                object.__setattr__(self, "_e", e)
                object.__setattr__(self, "last", None)
            def __getattr__(self, name):
                a = getattr(object.__getattribute__(self, "_e"), name)
                if callable(a):
                    def call(*args, **kw):
                        r = a(*args, **kw)
                        if hasattr(r, "then_inc"):
                            object.__setattr__(self, "last", r)
                        return r
                    return call
                return a

        def comp(eng, fn_body, waits=(), inc=True):
            semn = SEM_OF[eng]
            def fn(e, w, fn_body=fn_body, semn=semn, inc=inc):
                W(e, w)
                px = _Px(e)
                fn_body(px)
                if inc:
                    px.last.then_inc(sems[semn], 1)
            return sch.emit(eng, fn, waits=waits, inc=(semn if inc else None))

        for dst, src in [
            (s_idn[:], idn_bf), (s_idn16[:], idn_f16), (s_idn32[:], idn_f32),
            (s_sup[:], supper_bf),
            (s_sltb5[:], slt_b5), (s_i64[:], iota64r),
            (s_i128[:], iota128r), (s_i2048[:], iota2048r), (s_icol[:], iotacol),
            (s_th64[:], thresh64), (s_ones32[:], ones_f32), (s_onesbf[:], ones_bf),
            (s_ones16[:], ones_f16), (s_maskw[:], mask_w),
            (s_pc[:], pcols), (s_bff1[:], bff1c), (s_cc[:], constc),
            (s_ju[:], ju), (s_maskh[:], mask_half), (s_ival[:], ival),
        ]:
            dma("sync", dst, src)
        dma("gp", s_rot[:], rotc)
        dma("gp", s_ival16[:], ival16)
        statics_sd = sch.now("sd")
        statics_gd = sch.now("gd")

        def ln_tm(arena_src, arena_dst, waits, gcol=None, bcol=None):
            def mk_sq(a):
                for ti in range(NTILE):
                    a.activation(out=t32a[:], in_=arena_src[:, ti * E:(ti + 1) * E],
                                 func=AF.Square, accum_out=st[:, 16 + ti:17 + ti])
            sqt = comp("act", mk_sq, waits=waits)

            def mk_s1(v):
                for ti in range(NTILE):
                    v.tensor_reduce(out=st[:, ti:ti + 1],
                                    in_=arena_src[:, ti * E:(ti + 1) * E],
                                    axis=AX.X, op=ALU.add)
                v.drain()
                v.tensor_scalar(out=st[:, 0:16], in0=st[:, 0:16],
                                scalar1=1.0 / E, scalar2=None, op0=ALU.mult)
                v.tensor_scalar(out=st[:, 16:32], in0=st[:, 16:32],
                                scalar1=1.0 / E, scalar2=None, op0=ALU.mult)
                v.drain()
                v.tensor_tensor(out=st[:, 32:48], in0=st[:, 0:16],
                                in1=st[:, 0:16], op=ALU.mult)
                v.drain()
                v.tensor_tensor(out=st[:, 48:64], in0=st[:, 16:32],
                                in1=st[:, 32:48], op=ALU.subtract)
            s1t = comp("dve", mk_s1, waits=[sqt])

            def mk_sqrt(a):
                a.activation(out=st[:, 48:64], in_=st[:, 48:64], func=AF.Sqrt,
                             bias=s_cc[:, 0:1], scale=1.0)
            sqr = comp("act", mk_sqrt, waits=[s1t])

            def mk_apply(v):
                v.reciprocal(out=st[:, 48:64], in_=st[:, 48:64])
                v.drain()
                for ti in range(NTILE):
                    v.tensor_scalar(
                        out=arena_dst[:, ti * E:(ti + 1) * E],
                        in0=arena_src[:, ti * E:(ti + 1) * E],
                        scalar1=st[:, ti:ti + 1], scalar2=st[:, 48 + ti:49 + ti],
                        op0=ALU.subtract, op1=ALU.mult)
            return comp("dve", mk_apply, waits=[sqr])

        def tm_to_fm(src, dst, waits, gcol=None, bcol=None, banks=(0, 1)):
            evs = [None, None]
            first = True
            for fc in range(FC):
                for g in range(4):
                    bk = banks[(fc * 4 + g) % 2]
                    w = list(waits) if first else []
                    if evs[(fc * 4 + g) % 2] is not None:
                        w.append(evs[(fc * 4 + g) % 2])

                    def mk_tr(t, fc=fc, g=g, bk=bk, src=src):
                        for j in range(4):
                            ti = g * 4 + j
                            t.transpose(pbank16(bk)[:, j * 128:(j + 1) * 128],
                                        src[:, ti * E + fc * 128: ti * E + fc * 128 + 128],
                                        s_idn[:])
                    petgt = comp("pe", mk_tr, waits=w)

                    def mk_ev(v, fc=fc, g=g, bk=bk, dst=dst, gcol=gcol, bcol=bcol):
                        dap = dst[:, fc * T + g * 512: fc * T + (g + 1) * 512]
                        if gcol is None:
                            v.tensor_copy(out=dap, in_=pbank16(bk)[:, 0:512])
                        else:
                            v.tensor_scalar(out=dap, in0=pbank16(bk)[:, 0:512],
                                            scalar1=gcol(fc), scalar2=bcol(fc),
                                            op0=ALU.mult, op1=ALU.add)
                    evs[(fc * 4 + g) % 2] = comp("dve", mk_ev, waits=[petgt])
                    first = False
            return evs

        def fm_to_tm(src, dst, waits, banks=(0, 1)):
            evs = [None, None]
            first = True
            n = 0
            for ti in range(NTILE):
                for fg in range(2):
                    bk = banks[n % 2]
                    w = list(waits) if first else []
                    if evs[n % 2] is not None:
                        w.append(evs[n % 2])

                    def mk_tr(t, ti=ti, fg=fg, bk=bk, src=src):
                        for j in range(4):
                            fc = fg * 4 + j
                            t.transpose(pbank16(bk)[:, j * 128:(j + 1) * 128],
                                        src[:, fc * T + ti * 128: fc * T + (ti + 1) * 128],
                                        s_idn[:])
                    petgt = comp("pe", mk_tr, waits=w)

                    def mk_ev(v, ti=ti, fg=fg, bk=bk, dst=dst):
                        dap = dst[:, ti * E + fg * 512: ti * E + (fg + 1) * 512]
                        v.tensor_copy(out=dap, in_=pbank16(bk)[:, 0:512])
                    evs[n % 2] = comp("dve", mk_ev, waits=[petgt])
                    first = False
                    n += 1
            return evs

        def proj_fm(w_ap, n_oc, n_ic, src_fm, ev_fn, waits, wbufs=(0, 1),
                    pbanks=(2, 3), src_tag=None):
            ldtgts = {}
            ldtgts[0] = dma("gp", WS[wbufs[0]][:, 0:n_ic * 128],
                            w_ap[0], waits=[("pe", sch.now("pe"))])
            evtg = [None, None]
            mmtg = {}
            for oc in range(n_oc):
                if oc + 1 < n_oc:
                    w = []
                    if oc >= 1:
                        w.append(mmtg[(oc - 1, 3)])
                    ldtgts[oc + 1] = dma("gp",
                                         WS[wbufs[(oc + 1) % 2]][:, 0:n_ic * 128],
                                         w_ap[oc + 1], waits=w)
                for tsl in range(4):
                    bk = pbanks[(oc * 4 + tsl) % 2]
                    w = [ldtgts[oc]]
                    if oc == 0 and tsl == 0:
                        w += list(waits)
                    if evtg[(oc * 4 + tsl) % 2] is not None:
                        w.append(evtg[(oc * 4 + tsl) % 2])

                    def mk_mm(t, oc=oc, tsl=tsl, bk=bk):
                        wb = WS[wbufs[oc % 2]]
                        for ic in range(n_ic):
                            t.matmul(pbank(bk)[:],
                                     wb[:, ic * 128:(ic + 1) * 128],
                                     src_fm[:, ic * T + tsl * 512: ic * T + (tsl + 1) * 512],
                                     start=(ic == 0), stop=(ic == n_ic - 1))
                    mmtg[(oc, tsl)] = comp("pe", mk_mm, waits=w)

                    def mk_ev(v, oc=oc, tsl=tsl, bk=bk):
                        ev_fn(v, pbank(bk), oc, tsl)
                    evtg[(oc * 4 + tsl) % 2] = comp(
                        "dve", mk_ev, waits=[mmtg[(oc, tsl)]])
            return evtg

        def local_head(b, g, sub, qkv_done):
            r = 64 * sub
            fc = g
            QT = A3[:, 3264:4288]
            VT = A3[:, 2176:3264]
            KH = A3[0:64, 0:2176]
            ATN = A3[:, 4288:5312]
            PT = A3[:, 5312:5568]
            idn64 = s_idn[r:r + 64, r:r + 64]
            qh_ld = dma("sync", QH[:], QKG[r:r + 64, :],
                        waits=list(qkv_done) + [("pe", sch.now("pe"))])
            evq = {}
            for grp in range(4):
                w = list(qkv_done) if grp == 0 else []
                if grp >= 2:
                    w.append(evq[grp - 2])

                def mk_tr4(t, grp=grp, r=r):
                    bk = pbank16(4 + grp % 2)
                    for j in range(4):
                        ti = grp * 4 + j
                        t.transpose(bk[:, j * 64:(j + 1) * 64],
                                    QKG[r:r + 64, ti * 128:(ti + 1) * 128],
                                    idn64)
                ptg = comp("pe", mk_tr4, waits=w)

                def mk_evq(v, grp=grp):
                    v.tensor_copy(out=QT[:, grp * 256:(grp + 1) * 256],
                                  in_=pbank16(4 + grp % 2)[:, 0:256])
                evq[grp] = comp("dve", mk_evq, waits=[ptg])
            def mk_nrm(a):
                for ti in range(NTILE):
                    a.activation(out=t32a[:, 0:64], in_=QT[:, ti * 64:(ti + 1) * 64],
                                 func=AF.Square, accum_out=st[:, ti:ti + 1])
            nrm = comp("act", mk_nrm, waits=[evq[2], evq[3]])

            def mk_sqrt(a):
                a.activation(out=st[:, 16:32], in_=st[:, 0:16], func=AF.Sqrt,
                             bias=s_cc[:, 1:2], scale=1.0)
            sq = comp("act", mk_sqrt, waits=[nrm])

            def mk_inv(v):
                v.reciprocal(out=st[:, 16:32], in_=st[:, 16:32])
                v.drain()
                for ti in range(NTILE):
                    v.tensor_scalar(out=QT[:, ti * 64:(ti + 1) * 64],
                                    in0=QT[:, ti * 64:(ti + 1) * 64],
                                    scalar1=st[:, 16 + ti:17 + ti], scalar2=None,
                                    op0=ALU.mult)
            inv = comp("dve", mk_inv, waits=[sq])

            def mk_z(v):
                v.memset(KH[0:64, 0:128], 0.0)
            khz = comp("dve", mk_z, waits=())
            evk = {}
            for grp in range(4):
                w = [inv] if grp == 0 else []
                if grp >= 2:
                    w.append(evk[grp - 2])

                def mk_tr4(t, grp=grp):
                    bk = pbank16(4 + grp % 2)
                    for j in range(4):
                        ti = grp * 4 + j
                        t.transpose(bk[0:64, j * 128:(j + 1) * 128],
                                    QT[:, ti * 64:(ti + 1) * 64],
                                    s_idn[:])
                ptg = comp("pe", mk_tr4, waits=w)

                def mk_evk(v, grp=grp):
                    v.tensor_copy(out=KH[:, 128 + grp * 512:128 + (grp + 1) * 512],
                                  in_=pbank16(4 + grp % 2)[0:64, 0:512])
                evk[grp] = comp("dve", mk_evk, waits=[ptg])

            def mk_vz(v):
                v.memset(VT[:, 0:64], 0.0)
            vtz = comp("dve", mk_vz, waits=())
            evv = {}
            for grp in range(4):
                w = list(qkv_done) if grp == 0 else []
                if grp >= 2:
                    w.append(evv[grp - 2])

                def mk_tr4(t, grp=grp, r=r):
                    bk = pbank16(6 + grp % 2)
                    for j in range(4):
                        ti = grp * 4 + j
                        t.transpose(bk[:, j * 64:(j + 1) * 64],
                                    VG[r:r + 64, ti * 128:(ti + 1) * 128],
                                    idn64)
                ptg = comp("pe", mk_tr4, waits=w)

                def mk_evv(v, grp=grp):
                    v.tensor_copy(out=VT[:, 64 + grp * 256:64 + (grp + 1) * 256],
                                  in_=pbank16(6 + grp % 2)[:, 0:256])
                evv[grp] = comp("dve", mk_evv, waits=[ptg])

            lastdots = {}
            lastout = None
            for wg in range(8):
                bk = wg % 2
                w = [evk[3], qh_ld, ("dve", sch.now("dve"))] if wg == 0 else []
                if wg >= 2:
                    w.append(lastdots[wg % 2])

                def mk_dots(t, wg=wg, bk=bk, r=r):
                    for j in range(2):
                        wi = wg * 2 + j
                        t.matmul(pbank(bk)[:, j * 256:(j + 1) * 256],
                                 QH[:, wi * 128:(wi + 1) * 128],
                                 KH[:, wi * 128:wi * 128 + 256],
                                 start=True, stop=True)
                dtg = comp("pe", mk_dots, waits=w)

                def mk_sm(v, wg=wg, bk=bk):
                    for j in range(2):
                        wi = wg * 2 + j
                        v.scalar_tensor_tensor(
                            out=t32a[:, j * 256:(j + 1) * 256],
                            in0=pbank(bk)[:, j * 256:(j + 1) * 256],
                            scalar=0.125, in1=s_maskw[:],
                            op0=ALU.mult, op1=ALU.add)
                        if wi == 0:
                            v.memset(t32a[:, 0:128], MVAL)
                    v.drain()
                    v.tensor_reduce(out=st[:, 32:34],
                                    in_=t32a[:, 0:512].rearrange("p (w k) -> p w k", w=2),
                                    axis=AX.X, op=ALU.max)
                    v.drain()
                    v.tensor_tensor(
                        out=t32a[:, 0:512].rearrange("p (w k) -> p w k", w=2),
                        in0=t32a[:, 0:512].rearrange("p (w k) -> p w k", w=2),
                        in1=st[:, 32:34].unsqueeze(2).to_broadcast((128, 2, 256)),
                        op=ALU.subtract)
                smt = comp("dve", mk_sm, waits=[dtg])

                def mk_exp(a):
                    a.activation(out=t32b[:, 0:512], in_=t32a[:, 0:512], func=AF.Exp)
                ext = comp("act", mk_exp, waits=[smt])

                def mk_nrm2(v, wg=wg):
                    v.tensor_reduce(out=st[:, 34:36],
                                    in_=t32b[:, 0:512].rearrange("p (w k) -> p w k", w=2),
                                    axis=AX.X, op=ALU.add)
                    v.drain()
                    v.reciprocal(out=st[:, 34:36], in_=st[:, 34:36])
                    v.drain()
                    aslice = ATN[:, 0:512] if wg % 2 == 0 else ATN[:, 512:1024]
                    v.tensor_tensor(
                        out=aslice.rearrange("p (w k) -> p w k", w=2),
                        in0=t32b[:, 0:512].rearrange("p (w k) -> p w k", w=2),
                        in1=st[:, 34:36].unsqueeze(2).to_broadcast((128, 2, 256)),
                        op=ALU.mult)
                lastdots[wg % 2] = comp("dve", mk_nrm2, waits=[ext])

                for j in range(2):
                    wi = wg * 2 + j
                    aoff = (wg % 2) * 512 + j * 256

                    def mk_ptr(t, aoff=aoff):
                        t.transpose(pbank16(2)[:, 0:128], ATN[:, aoff:aoff + 128], s_idn[:])
                        t.transpose(pbank16(2)[:, 128:256], ATN[:, aoff + 128:aoff + 256], s_idn[:])
                    ptt = comp("pe", mk_ptr, waits=[lastdots[wg % 2]] + ([lastout] if lastout else []))

                    def mk_pte(v):
                        v.tensor_copy(out=PT[:], in_=pbank16(2)[:, 0:256])
                    pte = comp("dve", mk_pte, waits=[ptt])

                    def mk_out(t, wi=wi):
                        t.matmul(pbank(3)[0:64, 0:128], VT[:, wi * 64:(wi + 1) * 64],
                                 PT[:, 0:128], start=True, stop=False)
                        t.matmul(pbank(3)[0:64, 0:128], VT[:, (wi + 1) * 64:(wi + 2) * 64],
                                 PT[:, 128:256], start=False, stop=True)
                    ott = comp("pe", mk_out, waits=[pte])

                    def mk_oev(v, wi=wi, r=r, fc=fc):
                        v.tensor_copy(
                            out=AO[r:r + 64, fc * T + wi * 128: fc * T + (wi + 1) * 128],
                            in_=pbank(3)[0:64, 0:128])
                    lastout = comp("dve", mk_oev, waits=[ott])

        def lsh_head(b, g, sub, qkv_done):
            if klsh <= 0:
                return
            r = 64 * sub
            fc = g
            idn_r = s_idn[r:r + 64, r:r + 64]
            i16_00 = s_idn16[0:64, 0:64]
            i16_bb = s_idn16[64:128, 64:128]
            qh_ld = dma("sync", QH[:], QKG[r:r + 64, :],
                        waits=list(qkv_done) + [("pe", sch.now("pe"))])

            def rap(base, off, pattern):
                return bass.AP(base.tensor, base.offset + off,
                               [list(base.ap[0])] + [list(p) for p in pattern])

            A1f = A1[:].bitcast(FP16)
            A2f = A2[:].bitcast(FP16)
            A3f = A3[:].bitcast(FP16)
            A2_32 = A2[:].bitcast(F32)
            A3_32 = A3[:].bitcast(F32)
            TIMB = A1f[:, 0:8192]
            SRT = A1f[:, 8192:16384]
            SEL = [A2f[:, 0:4096], A2f[:, 4096:8192]]
            P1S = [A2f[:, 0:2048], A2f[:, 2048:4096]]
            SCR64 = A2f[0:64, 0:2048]
            INVB = A2f[:, 0:8192]
            VTMO = A2f[:, 8192:12352]
            XT = A2f[:, 12352:14400]
            TKW0 = A2f[:, 14400:15168]
            IKW0 = A2f[:, 15168:15936]
            PTB = A2f[:, 15936:16192]
            BOT = A2f[0:64, 16192:16320]
            KEYS = A2_32[:, 7200:7712]
            DLT = [A2f[:, 15424:15552], A2f[:, 15552:15680]]
            MVT = [A2f[:, 15680:15872], A2f[:, 15872:16064]]
            HALL = A3[:, 0:4096]
            YTM = A3f[:, 4096:8448]
            SM1 = A3_32[:, 4224:4992]
            SM2 = A3_32[:, 4992:5760]
            SM3 = A3_32[:, 5760:6528]
            SM12 = A3_32[:, 4224:5248]
            P4f = A3f[:, 13056:13824]
            cmb = 7296
            CMc = lambda c0, n: A3_32[:, cmb + c0:cmb + c0 + n]
            U_ = CMc(0, 64); D_ = CMc(64, 64); DT_ = CMc(128, 64); DL_ = CMc(192, 64)
            SRC_ = CMc(256, 64); HS_ = CMc(320, 64); LSE_ = CMc(384, 64); LG_ = CMc(448, 64)
            M_ = CMc(512, 16); SEu_ = CMc(528, 16); R_ = CMc(544, 16)
            NU_ = CMc(560, 16); IVU_ = CMc(576, 16)
            BH0_ = CMc(592, 16); BH1_ = CMc(608, 16)
            CPa_ = CMc(624, 16); CPb_ = CMc(640, 16)
            CQa_ = CMc(656, 16); CQb_ = CMc(672, 16)
            HC_ = CMc(688, 8)
            MSg_ = CMc(592, 64); RSg_ = CMc(656, 64); WP_ = CMc(720, 64)
            ITS_ = CMc(0, 64); ILS_ = CMc(64, 64)
            SRC16 = A3f[:, (cmb + 784) * 2:(cmb + 784) * 2 + 64]
            TM16a = A3f[:, (cmb + 816) * 2:(cmb + 816) * 2 + 16]
            bpflat = A1[:].bitcast(F32)[0:1, 0:2880]

            evh = {}
            for tp in range(8):
                w = [qh_ld, ("gd", statics_gd)] if tp == 0 else []
                if tp >= 1:
                    w.append(evh[tp - 1])

                def mk_rot(t, tp=tp):
                    for j in range(2):
                        ti = tp * 2 + j
                        t.matmul(pbank(j)[:, 0:128], QH[:, ti * 128:(ti + 1) * 128],
                                 s_rot[:], start=True, stop=True)
                rtg = comp("pe", mk_rot, waits=w)

                def mk_keys(v, tp=tp):
                    for j in range(2):
                        pv = pbank(j)[:, 0:128].rearrange("p (h i) -> p h i", h=4)
                        kv2 = KEYS[:, j * 256:(j + 1) * 256].rearrange(
                            "p (h i) -> p h i", h=4)
                        v.tensor_scalar(out=kv2[:, :, 0:32], in0=pv, scalar1=1.0,
                                        scalar2=None, op0=ALU.mult)
                        v.tensor_scalar(out=kv2[:, :, 32:64], in0=pv, scalar1=-1.0,
                                        scalar2=None, op0=ALU.mult)
                    v.drain()
                    for j in range(2):
                        v.tensor_reduce(
                            out=st[:, j * 4:j * 4 + 4],
                            in_=KEYS[:, j * 256:(j + 1) * 256].rearrange(
                                "p (h i) -> p h i", h=4),
                            axis=AX.X, op=ALU.max)
                    v.drain()
                    for j in range(2):
                        ti = tp * 2 + j
                        v.tensor_tensor(
                            out=HALL[:, ti * 256:(ti + 1) * 256].rearrange(
                                "p (h i) -> p h i", h=4),
                            in0=KEYS[:, j * 256:(j + 1) * 256].rearrange(
                                "p (h i) -> p h i", h=4),
                            in1=st[:, j * 4:j * 4 + 4].unsqueeze(2).to_broadcast(
                                (128, 4, 64)),
                            op=ALU.is_equal)
                    v.drain()
                    for j in range(2):
                        ti = tp * 2 + j
                        v.tensor_tensor(out=SM1[:, j * 256:(j + 1) * 256],
                                        in0=HALL[:, ti * 256:(ti + 1) * 256],
                                        in1=s_ju[:], op=ALU.mult)
                    v.drain()
                    for j in range(2):
                        ti = tp * 2 + j
                        v.tensor_reduce(
                            out=U_[:, ti * 4:ti * 4 + 4],
                            in_=SM1[:, j * 256:(j + 1) * 256].rearrange(
                                "p (h i) -> p h i", h=4),
                            axis=AX.X, op=ALU.add)
                evh[tp] = comp("dve", mk_keys, waits=[rtg])

            if klsh <= 1:
                return
            _dumped = (b == 0 and g == 2 and sub == 0)

            def dump(name, ap, waits):
                if not _dumped or kdump != name:
                    return
                n = ap.shape[-1] if len(ap.shape) == 2 else 8192
                p = ap.shape[0]
                q = "sync" if ap.dtype == F32 else "gp"
                dma(q, dbg[0:p, 0:n], ap, waits=waits)
            dump("U", U_[:], [("dve", sch.now("dve"))])

            def mk_hist(t):
                for B in range(NTILE):
                    hs = HALL[:, B * 256:(B + 1) * 256]
                    for h in range(4):
                        for half in range(2):
                            col = (2 * h + half) * 16 + B
                            t.matmul(pbank(2)[0:32, col:col + 1],
                                     hs[:, h * 64 + 32 * half:h * 64 + 32 * half + 32],
                                     s_onesbf[:, 0:1], start=True, stop=True)
            htg = comp("pe", mk_hist, waits=[evh[7]])

            def mk_bh(v):
                v.tensor_copy(out=SM3[0:32, 0:128], in_=pbank(2)[0:32, 0:128])
                v.drain()
                pk = SM3
                v.tensor_copy(out=BH0_[0:32, 0:16], in_=pk[0:32, 0:16])
                v.tensor_tensor(out=BH1_[0:32, 0:16], in0=pk[0:32, 16:32],
                                in1=pk[0:32, 32:48], op=ALU.add)
                v.tensor_tensor(out=CPa_[0:32, 0:16], in0=pk[0:32, 48:64],
                                in1=pk[0:32, 64:80], op=ALU.add)
                v.tensor_tensor(out=CPb_[0:32, 0:16], in0=pk[0:32, 80:96],
                                in1=pk[0:32, 96:112], op=ALU.add)
                v.tensor_copy(out=CQa_[0:32, 0:16], in_=pk[0:32, 112:128])
                v.drain()
                for bb, src_ in enumerate((BH0_, BH1_, CPa_, CPb_, CQa_)):
                    v.tensor_reduce(out=HC_[0:32, bb:bb + 1], in_=src_[0:32, 0:16],
                                    axis=AX.X, op=ALU.add)
                v.drain()
                for bb, src_ in enumerate((BH0_, BH1_, CPa_, CPb_, CQa_)):
                    v.tensor_copy(out=SM1[0:32, bb * 16:bb * 16 + 16],
                                  in_=src_[0:32, 0:16])
                v.drain()
                v.memset(SM2[0:32, 0:80], 0.0)
                v.drain()
                sm1v = SM1[0:32, 0:80].rearrange("p (b c) -> p b c", b=5)
                sm2v = SM2[0:32, 0:80].rearrange("p (b c) -> p b c", b=5)
                v.tensor_copy(out=sm2v[:, :, 1:16], in_=sm1v[:, :, 0:15])
                v.drain()
                a_, b_ = SM2, SM1
                for sh in (1, 2, 4, 8):
                    av = a_[0:32, 0:80].rearrange("p (b c) -> p b c", b=5)
                    bv = b_[0:32, 0:80].rearrange("p (b c) -> p b c", b=5)
                    v.tensor_copy(out=bv[:, :, 0:sh], in_=av[:, :, 0:sh])
                    v.tensor_tensor(out=bv[:, :, sh:16], in0=av[:, :, sh:16],
                                    in1=av[:, :, 0:16 - sh], op=ALU.add)
                    v.drain()
                    a_, b_ = b_, a_
            bht = comp("dve", mk_bh, waits=[htg])

            def mk_pe2(t):
                for bb in range(5):
                    t.matmul(pbank(5)[0:1, 0:160], HC_[0:32, bb:bb + 1],
                             s_sltb5[0:32, bb * 160:(bb + 1) * 160],
                             start=(bb == 0), stop=(bb == 4))
            pet = comp("pe", mk_pe2, waits=[bht])

            def mk_bp(v):
                v.tensor_copy(out=bpflat[0:1, 2560:2720], in_=pbank(5)[0:1, 0:160])
            bpt = comp("dve", mk_bp, waits=[pet])
            dts = []
            for bb in range(5):
                dts.append(dma("sync",
                               bass.AP(dscr, bb * 512, [[16, 32], [1, 16]]),
                               SM2[0:32, bb * 16:bb * 16 + 16], waits=[bht]))
            d1 = dma("sync", bpflat[0:1, 0:2560],
                     bass.AP(dscr, 0, [[1, 2560]]), waits=dts)
            d2 = d1
            if _dumped and kdump == "BP":
                dma("sync", dbg[0:1, 0:2880], bpflat[0:1, 0:2880],
                    waits=[d1, bpt])
            if _dumped and kdump == "HC":
                dma("sync", dbg[0:32, 0:5], HC_[0:32, 0:5], waits=[bht])

            exv = {}
            for B in range(NTILE):
                bk = 3 + B % 2
                w = [bpt, d1, d2] if B == 0 else []
                if B >= 2:
                    w.append(exv[B - 2])

                def mk_cum(t, B=B, bk=bk):
                    hs = HALL[:, B * 256:(B + 1) * 256]
                    for h in range(4):
                        t.matmul(pbank(bk)[:, 32 + 64 * h:96 + 64 * h], s_sup[:],
                                 hs[:, h * 64:(h + 1) * 64],
                                 start=True, stop=True)
                ctg0 = comp("pe", mk_cum, waits=w)

                def mk_cumz(v, bk=bk):
                    v.memset(pbank(bk)[:, 0:32], 0.0)
                    v.memset(pbank(bk)[:, 288:320], 0.0)
                cz = comp("dve", mk_cumz, waits=[ctg0])

                def mk_cum2(t, B=B, bk=bk):
                    t.matmul(pbank(bk)[:, 320:480], s_ones32[0:1, :],
                             rap(bpflat, B, [[16, 160]]),
                             start=True, stop=False)
                    t.matmul(pbank(bk)[:, 320:480], s_ones32[0:1, :],
                             bpflat[0:1, 2560:2720], start=False, stop=True)
                ctg = comp("pe", mk_cum2, waits=[cz])

                def mk_ext(v, B=B, bk=bk):
                    if B == 0 and kdump == "PB0":
                        v.tensor_copy(out=SM3[:, 0:480], in_=pbank(bk)[:, 0:480])
                    own = rap(pbank(bk), 32, [[64, 4], [1, 64]])
                    nblo = rap(pbank(bk), 0, [[64, 4], [1, 32]])
                    nbhi = rap(pbank(bk), 96, [[64, 4], [1, 32]])
                    pebp = rap(pbank(bk), 320, [[32, 4], [1, 64]])
                    sm2v = SM2[:, 0:256].rearrange("p (h u) -> p h u", h=4)
                    v.tensor_copy(out=sm2v, in_=pebp)
                    v.drain()
                    v.tensor_tensor(out=sm2v, in0=sm2v, in1=own, op=ALU.add)
                    v.drain()
                    v.tensor_tensor(
                        out=SM2[:, 0:256].rearrange("p (h u) -> p h u", h=4)[:, :, 0:32],
                        in0=SM2[:, 0:256].rearrange("p (h u) -> p h u", h=4)[:, :, 0:32],
                        in1=nblo, op=ALU.add)
                    v.tensor_tensor(
                        out=SM2[:, 0:256].rearrange("p (h u) -> p h u", h=4)[:, :, 32:64],
                        in0=SM2[:, 0:256].rearrange("p (h u) -> p h u", h=4)[:, :, 32:64],
                        in1=nbhi, op=ALU.add)
                    v.drain()
                    v.tensor_tensor(
                        out=SM1[:, 0:256].rearrange("p (h u) -> p h u", h=4),
                        in0=HALL[:, B * 256:(B + 1) * 256].rearrange(
                            "p (h u) -> p h u", h=4),
                        in1=SM2[:, 0:256].rearrange("p (h u) -> p h u", h=4),
                        op=ALU.mult)
                    v.drain()
                    v.tensor_reduce(out=D_[:, B * 4:B * 4 + 4],
                                    in_=SM1[:, 0:256].rearrange(
                                        "p (h u) -> p h u", h=4),
                                    axis=AX.X, op=ALU.add)
                    if B == 0 and kdump == "V0":
                        v.drain()
                        v.tensor_copy(out=SM3[:, 0:256], in_=SM2[:, 0:256])
                exv[B] = comp("dve", mk_ext, waits=[ctg])
                if B == 0 and kdump in ("V0", "PB0") and _dumped:
                    n_ = 256 if kdump == "V0" else 480
                    exv[B] = dma("sync", dbg[0:128, 0:n_], SM3[:, 0:n_],
                                 waits=[exv[B]])

            dump("D0", D_[:], [exv[15]])
            def mk_st2(v):
                uv = U_[:].rearrange("p (t h) -> p t h", h=4)
                dv_ = D_[:].rearrange("p (t h) -> p t h", h=4)
                k = 0
                for h1 in range(4):
                    for h0 in range(h1):
                        v.tensor_tensor(out=SM1[:, k * 16:(k + 1) * 16],
                                        in0=uv[:, :, h1], in1=uv[:, :, h0],
                                        op=ALU.is_equal)
                        k += 1
                v.drain()
                k = 0
                for h1 in range(4):
                    for h0 in range(h1):
                        v.tensor_tensor(out=dv_[:, :, h1], in0=dv_[:, :, h1],
                                        in1=SM1[:, k * 16:(k + 1) * 16], op=ALU.add)
                        v.drain()
                        k += 1
                for grp in range(4):
                    v.tensor_tensor(
                        out=SM12[:, 0:1024].rearrange("p (c t) -> p c t", c=16),
                        in0=D_[:, grp * 16:(grp + 1) * 16].unsqueeze(2).to_broadcast(
                            (128, 16, 64)),
                        in1=s_th64[:].unsqueeze(1).to_broadcast((128, 16, 64)),
                        op=ALU.is_ge)
                    v.drain()
                    v.tensor_reduce(out=DT_[:, grp * 16:(grp + 1) * 16],
                                    in_=SM12[:, 0:1024].rearrange(
                                        "p (c t) -> p c t", c=16),
                                    axis=AX.X, op=ALU.add)
                    v.drain()
                v.tensor_scalar(out=DT_[:], in0=DT_[:], scalar1=-1.0,
                                scalar2=None, op0=ALU.add)
                v.drain()
                v.scalar_tensor_tensor(out=DL_[:], in0=DT_[:], scalar=-128.0,
                                       in1=D_[:], op0=ALU.mult, op1=ALU.add)
            st2t = comp("dve", mk_st2, waits=[exv[14], exv[15]])

            dump("D", D_[:], [st2t])
            dump("DT", DT_[:], [st2t])
            dump("DL", DL_[:], [st2t])
            if klsh <= 2:
                return
            evx = {}
            for grp in range(4):
                w = [qh_ld] + list(qkv_done) if grp == 0 else []
                if grp >= 1:
                    w.append(evx[grp - 1])

                def mk_xtr(t, grp=grp):
                    bkq = pbank16(6)
                    bkv = pbank16(7)
                    for j in range(4):
                        ti = grp * 4 + j
                        t.transpose(bkq[:, j * 64:(j + 1) * 64],
                                    QH[:, ti * 128:(ti + 1) * 128], s_idn[0:64, 0:64])
                    for j in range(4):
                        ti = grp * 4 + j
                        t.transpose(bkv[:, j * 64:(j + 1) * 64],
                                    VG[r:r + 64, ti * 128:(ti + 1) * 128], idn_r)
                xtg = comp("pe", mk_xtr, waits=w)

                def mk_xev(v, grp=grp):
                    xv = bass.AP(XT.tensor, XT.offset + grp * 512,
                                 [list(XT.ap[0]), [128, 4], [1, 64]])
                    xv2 = bass.AP(XT.tensor, XT.offset + grp * 512 + 64,
                                  [list(XT.ap[0]), [128, 4], [1, 64]])
                    v.tensor_copy(out=xv, in_=pbank16(6)[:, 0:256].rearrange(
                        "p (j c) -> p j c", j=4))
                    v.tensor_copy(out=xv2, in_=pbank16(7)[:, 0:256].rearrange(
                        "p (j c) -> p j c", j=4))
                evx[grp] = comp("dve", mk_xev, waits=[xtg])

            if klsh <= 3:
                return
            def mk_nu(a):
                for ti in range(NTILE):
                    a.activation(out=t32a[:, 0:64], in_=XT[:, ti * 128:ti * 128 + 64],
                                 func=AF.Square, accum_out=NU_[:, ti:ti + 1])
            nut = comp("act", mk_nu, waits=[evx[2], evx[3]])

            def mk_ivs(a):
                a.activation(out=IVU_[:], in_=NU_[:], func=AF.Sqrt,
                             bias=s_cc[:, 1:2], scale=1.0)
            ivs = comp("act", mk_ivs, waits=[nut])

            def mk_ivr(v):
                v.reciprocal(out=IVU_[:], in_=IVU_[:])
            ivr = comp("dve", mk_ivr, waits=[ivs])

            if klsh <= 4:
                return
            if klsh <= 5:
                return
            mmsr = None
            srmm = {}
            dvp = {}
            for k in range(64):
                ti, h = divmod(k, 4)
                w = [st2t, ivr] if k == 0 else []
                if k >= 2:
                    w.append(srmm[k - 2])

                def mk_dlmv(v, k=k, ti=ti, h=h):
                    dcol = DL_[:, ti * 4 + h:ti * 4 + h + 1]
                    tcol = DT_[:, ti * 4 + h:ti * 4 + h + 1]
                    v.tensor_scalar(out=DLT[k % 2][:], in0=s_i128[:],
                                    scalar1=dcol, scalar2=None, op0=ALU.is_equal)
                    mv = MVT[k % 2]
                    v.tensor_scalar(out=mv[:, 0:64], in0=s_i64[:],
                                    scalar1=tcol, scalar2=s_ival[:, ti:ti + 1],
                                    op0=ALU.is_equal, op1=ALU.mult)
                    v.tensor_scalar(out=mv[:, 64:128], in0=s_i64[:],
                                    scalar1=tcol, scalar2=float(h),
                                    op0=ALU.is_equal, op1=ALU.mult)
                    v.tensor_scalar(out=mv[:, 128:192], in0=s_i64[:],
                                    scalar1=tcol, scalar2=IVU_[:, ti:ti + 1],
                                    op0=ALU.is_equal, op1=ALU.mult)
                dvp[k] = comp("dve", mk_dlmv, waits=w)

                def mk_srmm(t, k=k):
                    t.matmul(pbank(6)[:, 0:192], DLT[k % 2][:], MVT[k % 2][:],
                             start=(k == 0), stop=(k == 63))
                srmm[k] = comp("pe", mk_srmm, waits=[dvp[k]])
                mmsr = srmm[k]

            def mk_srev(v):
                v.tensor_copy(out=SRC_[:], in_=pbank(6)[:, 0:64])
                v.tensor_copy(out=HS_[:], in_=pbank(6)[:, 64:128])
                v.tensor_copy(out=WP_[:], in_=pbank(6)[:, 128:192])
            srev = comp("dve", mk_srev, waits=[mmsr])
            dump("SRC", SRC_[:], [srev])
            dump("HS", HS_[:], [srev])
            dump("IVW", WP_[:], [srev])

            if klsh <= 6:
                return
            TROW = A1f[0:1, 8192:16384]

            def mk_str(t):
                t.transpose(pbank(2)[0:64, 0:128], SRC_[:], s_idn32[:])
            strt = comp("pe", mk_str, waits=[srev])

            def mk_sre(v):
                v.tensor_copy(out=SM1[0:64, 0:128], in_=pbank(2)[0:64, 0:128])
            sre = comp("dve", mk_sre, waits=[strt])
            ta1 = dma("sync", bass.AP(dscr, 4096, [[128, 64], [1, 128]]),
                      SM1[0:64, 0:128], waits=[sre])
            ta2 = dma("gp", TROW, bass.AP(dscr, 4096, [[1, 8192]]), waits=[ta1])
            tevs = []
            tmmp = {}
            for n in range(16):
                w = [ta2] if n == 0 else []
                if n >= 2:
                    w.append(tevs[n - 2])

                def mk_tmm(t, n=n):
                    t.matmul(pbank(2 + n % 2)[:, 0:512], s_ones16[0:1, 0:128],
                             TROW[0:1, n * 512:(n + 1) * 512], start=True, stop=True)
                tmmp[n] = comp("pe", mk_tmm, waits=w)

                def mk_tev(v, n=n):
                    v.tensor_copy(out=TIMB[:, n * 512:(n + 1) * 512],
                                  in_=pbank(2 + n % 2)[:, 0:512])
                tevs.append(comp("dve", mk_tev, waits=[tmmp[n]]))
            tb = tevs[-1]

            if klsh <= 7:
                return
            gevs = []
            segather = {}
            for h2 in range(2):
                mmg = {}
                for ic in range(16):
                    w = [tb, ("dve", sch.now("dve"))] if (h2 == 0 and ic == 0) else []
                    if h2 == 1 and ic == 0:
                        w = [x for x in gevs] + [("pe", sch.now("pe"))]
                    if ic >= 2:
                        w.append(mmg[ic - 2])

                    def mk_sel(v, h2=h2, ic=ic):
                        v.tensor_scalar(out=SEL[ic % 2][:],
                                        in0=TIMB[:, h2 * 4096:(h2 + 1) * 4096],
                                        scalar1=s_ival[:, ic:ic + 1], scalar2=0.0,
                                        op0=ALU.subtract, op1=ALU.is_equal)
                    seltg = comp("dve", mk_sel, waits=w)

                    def mk_gmm(t, h2=h2, ic=ic):
                        for nb in range(8):
                            t.matmul(gbank(nb), XT[:, ic * 128:(ic + 1) * 128],
                                     SEL[ic % 2][:, nb * 512:(nb + 1) * 512],
                                     start=(ic == 0), stop=(ic == 15))
                    mmg[ic] = comp("pe", mk_gmm, waits=[seltg])
                gevs = []
                for nb in range(8):
                    def mk_gev(v, h2=h2, nb=nb):
                        v.tensor_copy(
                            out=SRT[:, h2 * 4096 + nb * 512:h2 * 4096 + (nb + 1) * 512],
                            in_=gbank(nb))
                    gevs.append(comp("dve", mk_gev, waits=[mmg[15]]))

            if klsh <= 8:
                return
            dump("SRTQ", SRT[0:64, :], [("dve", sch.now("dve"))])
            IROW = A2f[0:1, 0:8192]

            def mk_itr(t):
                t.transpose(pbank(2)[0:64, 0:128], WP_[:], s_idn32[:])
            itrt = comp("pe", mk_itr, waits=[srev] + gevs)

            def mk_ire(v):
                v.tensor_copy(out=SM1[0:64, 0:128], in_=pbank(2)[0:64, 0:128])
            ire = comp("dve", mk_ire, waits=[itrt])
            ia1 = dma("sync", bass.AP(dscr, 16384, [[128, 64], [1, 128]]),
                      SM1[0:64, 0:128], waits=[ire])
            ia2 = dma("gp", IROW, bass.AP(dscr, 16384, [[1, 8192]]), waits=[ia1])
            ievs = []
            immp = {}
            for n in range(16):
                w = [ia2] if n == 0 else []
                if n >= 2:
                    w.append(ievs[n - 2])

                def mk_imm(t, n=n):
                    t.matmul(pbank(2 + n % 2)[:, 0:512], s_ones16[0:1, 0:128],
                             IROW[0:1, n * 512:(n + 1) * 512], start=True, stop=True)
                immp[n] = comp("pe", mk_imm, waits=w)

                def mk_iev(v, n=n):
                    v.tensor_copy(out=INVB[:, n * 512:(n + 1) * 512],
                                  in_=pbank(2 + n % 2)[:, 0:512])
                ievs.append(comp("dve", mk_iev, waits=[immp[n]]))
            ivb = ievs[-1]

            if klsh <= 9:
                return
            evv = {}
            for grp in range(9):
                tiles = range(grp * 8, min(65, (grp + 1) * 8))
                if not tiles:
                    continue
                w = list(gevs) if grp == 0 else []
                if grp >= 2:
                    w.append(evv[grp - 2])

                def mk_vtr(t, grp=grp, tiles=tuple(tiles)):
                    bk = pbankh(6 + grp % 2)
                    for j, tt in enumerate(tiles):
                        if tt == 0:
                            t.transpose(bk[0:64, j * 64:(j + 1) * 64],
                                        SRT[64:128, 8128:8192], i16_bb)
                        elif tt == 64:
                            t.transpose(bk[:, j * 64:(j + 1) * 64],
                                        SRT[64:128, 8064:8192], i16_bb)
                        else:
                            t.transpose(bk[:, j * 64:(j + 1) * 64],
                                        SRT[64:128, tt * 128 - 64:tt * 128 + 64],
                                        i16_bb)
                vtg = comp("pe", mk_vtr, waits=w)

                def mk_vev(v, grp=grp, tiles=tuple(tiles)):
                    n = len(tiles)
                    v.tensor_copy(
                        out=VTMO[:, tiles[0] * 64:tiles[0] * 64 + n * 64],
                        in_=pbankh(6 + grp % 2)[:, 0:n * 64])
                evv[grp] = comp("dve", mk_vev, waits=[vtg])
            def mk_v0(t):
                t.transpose(pbankh(5)[0:64, 0:64], SRT[64:128, 0:64], i16_bb)
            v0t = comp("pe", mk_v0, waits=[evv[0]])

            def mk_v0e(v):
                v.tensor_copy(out=BOT[0:64, 0:64], in_=pbankh(5)[0:64, 0:64])
            v0e = comp("dve", mk_v0e, waits=[v0t])
            v0d = dma("sync", VTMO[64:128, 0:64], BOT[0:64, 0:64], waits=[v0e])

            def mk_w0(v):
                v.tensor_copy(out=TKW0[:, 0:64], in_=TIMB[:, 8128:8192])
                v.tensor_copy(out=TKW0[:, 64:192], in_=TIMB[:, 0:128])
                v.tensor_copy(out=rap(TKW0, 192, [[192, 3], [1, 192]]),
                              in_=rap(TIMB, 64, [[128, 3], [1, 192]]))
                v.tensor_copy(out=IKW0[:, 0:64], in_=INVB[:, 8128:8192])
                v.tensor_copy(out=IKW0[:, 64:192], in_=INVB[:, 0:128])
                v.tensor_copy(out=rap(IKW0, 192, [[192, 3], [1, 192]]),
                              in_=rap(INVB, 64, [[128, 3], [1, 192]]))
            w0t = comp("dve", mk_w0, waits=[ivb, tb])

            if klsh <= 10:
                return
            lastb = None
            for b4 in range(16):
                w = [w0t, v0d, ("dve", sch.now("dve")), ("gd", sch.now("gd"))] if b4 == 0 else []
                if lastb is not None:
                    w.append(lastb)

                def mk_dots(t, b4=b4):
                    for j in range(4):
                        gq = b4 * 4 + j
                        dst = pcb(b4 % 2)
                        if gq == 0:
                            t.matmul(dst[:, 0:64], SRT[0:64, 0:128],
                                     SRT[0:64, 8128:8192], start=True, stop=True)
                            t.matmul(dst[:, 64:192], SRT[0:64, 0:128],
                                     SRT[0:64, 0:128], start=True, stop=True)
                        else:
                            t.matmul(dst[:, j * 256:j * 256 + 192],
                                     SRT[0:64, gq * 128:(gq + 1) * 128],
                                     SRT[0:64, gq * 128 - 64:gq * 128 + 128],
                                     start=True, stop=True)
                dtg = comp("pe", mk_dots, waits=w)

                def mk_msk(v, b4=b4):
                    pcv = rap(pcb(b4 % 2), 0, [[256, 4], [1, 192]])
                    if b4 == 0:
                        tkv = TKW0[:].rearrange("p (g k) -> p g k", g=4)
                        ikv = IKW0[:].rearrange("p (g k) -> p g k", g=4)
                    else:
                        tkv = rap(TIMB, b4 * 512 - 64, [[128, 4], [1, 192]])
                        ikv = rap(INVB, b4 * 512 - 64, [[128, 4], [1, 192]])
                    tqv = SRC_[:, b4 * 4:b4 * 4 + 4].unsqueeze(2).to_broadcast(
                        (128, 4, 192))
                    sm1v = SM1[:].rearrange("p (g k) -> p g k", g=4)
                    sm2v = SM2[:].rearrange("p (g k) -> p g k", g=4)
                    sm3v = SM3[:].rearrange("p (g k) -> p g k", g=4)
                    v.tensor_tensor(out=sm1v, in0=pcv, in1=ikv, op=ALU.mult)
                    v.tensor_tensor(out=sm2v, in0=tkv, in1=tqv, op=ALU.is_gt)
                    v.tensor_tensor(out=sm3v, in0=tkv, in1=tqv, op=ALU.is_equal)
                    v.scalar_tensor_tensor(
                        out=sm1v, in0=sm1v, scalar=0.125,
                        in1=s_maskh[:].unsqueeze(1).to_broadcast((128, 4, 192)),
                        op0=ALU.mult, op1=ALU.add)
                    v.tensor_scalar(out=SM3[:], in0=SM3[:], scalar1=TSV,
                                    scalar2=None, op0=ALU.mult)
                    v.drain()
                    v.scalar_tensor_tensor(out=SM2[:], in0=SM2[:], scalar=MVAL,
                                           in1=SM3[:], op0=ALU.mult, op1=ALU.add)
                    v.drain()
                    v.tensor_tensor(out=SM1[:], in0=SM1[:], in1=SM2[:], op=ALU.add)
                    v.drain()
                    v.tensor_reduce(out=st[:, 36:40], in_=sm1v, axis=AX.X,
                                    op=ALU.max)
                    v.drain()
                    v.tensor_tensor(out=sm1v, in0=sm1v,
                                    in1=st[:, 36:40].unsqueeze(2).to_broadcast(
                                        (128, 4, 192)),
                                    op=ALU.subtract)
                mtg = comp("dve", mk_msk, waits=[dtg])

                def mk_exp(a):
                    a.activation(out=SM2[:], in_=SM1[:], func=AF.Exp)
                etg = comp("act", mk_exp, waits=[mtg])

                def mk_p4(v, b4=b4):
                    sm2v = SM2[:].rearrange("p (g k) -> p g k", g=4)
                    v.tensor_reduce(out=st[:, 40:44], in_=sm2v, axis=AX.X,
                                    op=ALU.add)
                    v.drain()
                    v.reciprocal(out=st[:, 44:48], in_=st[:, 40:44])
                    v.drain()
                    v.tensor_tensor(out=P4f[:].rearrange("p (g k) -> p g k", g=4),
                                    in0=sm2v,
                                    in1=st[:, 44:48].unsqueeze(2).to_broadcast(
                                        (128, 4, 192)),
                                    op=ALU.mult)
                p4t = comp("dve", mk_p4, waits=[etg])

                def mk_lse(a):
                    a.activation(out=st[:, 48:52], in_=st[:, 40:44], func=AF.Ln)
                lnt = comp("act", mk_lse, waits=[p4t])

                def mk_lse2(v, b4=b4):
                    v.tensor_tensor(out=LSE_[:, b4 * 4:b4 * 4 + 4],
                                    in0=st[:, 36:40], in1=st[:, 48:52], op=ALU.add)
                lst = comp("dve", mk_lse2, waits=[lnt])

                for j in range(4):
                    gq = b4 * 4 + j

                    def mk_ptr(t, j=j):
                        t.transpose(pbankh(5)[:, 0:128],
                                    P4f[:, j * 192:j * 192 + 128], s_idn16[:])
                        t.transpose(pbankh(5)[0:64, 128:256],
                                    P4f[:, j * 192 + 128:j * 192 + 192], s_idn16[:])
                    ptg = comp("pe", mk_ptr, waits=[p4t])

                    def mk_pte(v):
                        v.tensor_copy(out=PTB[:], in_=pbankh(5)[:, 0:256])
                    pte = comp("dve", mk_pte, waits=[ptg])

                    def mk_bo(t, gq=gq):
                        t.matmul(pbank(3)[0:64, 0:128], VTMO[:, gq * 64:(gq + 1) * 64],
                                 PTB[:, 0:128], start=True, stop=False)
                        t.matmul(pbank(3)[0:64, 0:128],
                                 VTMO[0:64, (gq + 1) * 64:(gq + 2) * 64],
                                 PTB[0:64, 128:256], start=False, stop=True)
                    bot_ = comp("pe", mk_bo, waits=[pte])

                    def mk_boe(v):
                        v.tensor_copy(out=BOT[0:64, 0:128], in_=pbank(3)[0:64, 0:128])
                    boe = comp("dve", mk_boe, waits=[bot_])

                    def mk_bot(t):
                        t.transpose(pbankh(2)[:, 0:64], BOT[0:64, 0:128], i16_00)
                    btt = comp("pe", mk_bot, waits=[boe])

                    def mk_ye(v, gq=gq):
                        v.tensor_copy(out=YTM[:, gq * 68:gq * 68 + 64],
                                      in_=pbankh(2)[:, 0:64])
                        v.tensor_scalar(out=YTM[:, gq * 68 + 64:gq * 68 + 68],
                                        in0=s_i64[:, 0:4],
                                        scalar1=HS_[:, gq:gq + 1],
                                        scalar2=LSE_[:, gq:gq + 1],
                                        op0=ALU.is_equal, op1=ALU.mult)
                    lastb = comp("dve", mk_ye, waits=[btt, lst])

            if klsh <= 11:
                return
            dump("LSE", LSE_[:], [("dve", sch.now("dve"))])
            dump("YT", YTM[:, 0:4352], [("dve", sch.now("dve"))])
            def mk_its(v):
                for grp in range(4):
                    v.tensor_tensor(
                        out=SM12[:, 0:256].rearrange("p (c t) -> p c t", c=16),
                        in0=SRC_[:, grp * 16:(grp + 1) * 16].unsqueeze(2).to_broadcast(
                            (128, 16, 16)),
                        in1=s_th64[:, 0:16].unsqueeze(1).to_broadcast((128, 16, 16)),
                        op=ALU.is_ge)
                    v.drain()
                    v.tensor_reduce(out=ITS_[:, grp * 16:(grp + 1) * 16],
                                    in_=SM12[:, 0:256].rearrange(
                                        "p (c t) -> p c t", c=16),
                                    axis=AX.X, op=ALU.add)
                    v.drain()
                v.tensor_scalar(out=ITS_[:], in0=ITS_[:], scalar1=-1.0,
                                scalar2=None, op0=ALU.add)
                v.drain()
                v.scalar_tensor_tensor(out=ILS_[:], in0=ITS_[:], scalar=-128.0,
                                       in1=SRC_[:], op0=ALU.mult, op1=ALU.add)
            itst = comp("dve", mk_its, waits=[lastb])

            mml = None
            lmm = {}
            dvl = {}
            for c in range(64):
                w = [itst] if c == 0 else []
                if c >= 2:
                    w.append(lmm[c - 2])

                def mk_dlp(v, c=c):
                    v.tensor_scalar(out=DLT[c % 2][:], in0=s_i128[:],
                                    scalar1=ILS_[:, c:c + 1], scalar2=None,
                                    op0=ALU.is_equal)
                    v.tensor_scalar(out=TM16a[:], in0=s_i64[:, 0:16],
                                    scalar1=ITS_[:, c:c + 1], scalar2=None,
                                    op0=ALU.is_equal)
                    v.drain()
                    v.tensor_tensor(
                        out=MVT[c % 2][:, 0:64].rearrange("p (t h) -> p t h", h=4),
                        in0=TM16a[:].unsqueeze(2).to_broadcast((128, 16, 4)),
                        in1=YTM[:, c * 68 + 64:c * 68 + 68].unsqueeze(1).to_broadcast(
                            (128, 16, 4)),
                        op=ALU.mult)
                dvl[c] = comp("dve", mk_dlp, waits=w)

                def mk_lmm(t, c=c):
                    t.matmul(pbank(6)[:, 0:64], DLT[c % 2][:], MVT[c % 2][:, 0:64],
                             start=(c == 0), stop=(c == 63))
                lmm[c] = comp("pe", mk_lmm, waits=[dvl[c]])
                mml = lmm[c]

            def mk_lgev(v):
                v.tensor_copy(out=LG_[:], in_=pbank(6)[:, 0:64])
                v.drain()
                v.tensor_reduce(out=M_[:], in_=LG_[:].rearrange(
                    "p (t h) -> p t h", h=4), axis=AX.X, op=ALU.max)
                v.drain()
                v.tensor_tensor(out=SM1[:, 0:64].rearrange("p (t h) -> p t h", h=4),
                                in0=LG_[:].rearrange("p (t h) -> p t h", h=4),
                                in1=M_[:].unsqueeze(2).to_broadcast((128, 16, 4)),
                                op=ALU.subtract)
            lgt = comp("dve", mk_lgev, waits=[mml])

            def mk_lex(a):
                a.activation(out=SM2[:, 0:64], in_=SM1[:, 0:64], func=AF.Exp)
            lxt = comp("act", mk_lex, waits=[lgt])

            def mk_ser(v):
                v.tensor_reduce(out=SEu_[:], in_=SM2[:, 0:64].rearrange(
                    "p (t h) -> p t h", h=4), axis=AX.X, op=ALU.add)
                v.drain()
                v.reciprocal(out=R_[:], in_=SEu_[:])
                v.drain()
                v.tensor_tensor(out=SM1[:, 0:64].rearrange("p (t h) -> p t h", h=4),
                                in0=SM2[:, 0:64].rearrange("p (t h) -> p t h", h=4),
                                in1=R_[:].unsqueeze(2).to_broadcast((128, 16, 4)),
                                op=ALU.mult)
            sert = comp("dve", mk_ser, waits=[lxt])

            if klsh <= 12:
                return
            dump("LG", LG_[:], [("dve", sch.now("dve"))])
            mmr2 = None
            rmm = {}
            dvr = {}
            for k in range(64):
                ti, h = divmod(k, 4)
                w = [sert] if k == 0 else []
                if k >= 2:
                    w.append(rmm[k - 2])

                def mk_dmr(v, k=k, ti=ti, h=h):
                    dcol = DL_[:, ti * 4 + h:ti * 4 + h + 1]
                    tcol = DT_[:, ti * 4 + h:ti * 4 + h + 1]
                    v.tensor_scalar(out=DLT[k % 2][:], in0=s_i128[:],
                                    scalar1=dcol, scalar2=None, op0=ALU.is_equal)
                    v.tensor_scalar(out=MVT[k % 2][:, 0:64], in0=s_i64[:],
                                    scalar1=tcol,
                                    scalar2=SM1[:, ti * 4 + h:ti * 4 + h + 1],
                                    op0=ALU.is_equal, op1=ALU.mult)
                dvr[k] = comp("dve", mk_dmr, waits=w)

                def mk_rmm(t, k=k):
                    t.matmul(pbank(7)[:, 0:64], DLT[k % 2][:], MVT[k % 2][:, 0:64],
                             start=(k == 0), stop=(k == 63))
                rmm[k] = comp("pe", mk_rmm, waits=[dvr[k]])
                mmr2 = rmm[k]

            def mk_wp2(v):
                v.tensor_copy(out=WP_[:], in_=pbank(7)[:, 0:64])
                v.drain()
                ytv = rap(YTM, 0, [[68, 64], [1, 64]])
                v.tensor_tensor(out=ytv, in0=ytv,
                                in1=WP_[:].unsqueeze(2).to_broadcast((128, 64, 64)),
                                op=ALU.mult)
            wp2 = comp("dve", mk_wp2, waits=[mmr2])

            if klsh <= 13:
                return
            dump("WP", WP_[:], [wp2])
            mms = {}
            dvs = {}
            for sc in range(64):
                w = [wp2] if sc == 0 else []
                if sc >= 2:
                    w.append(mms[sc - 2])

                def mk_p1s(v, sc=sc):
                    v.tensor_scalar(out=P1S[sc % 2][:], in0=s_i2048[:],
                                    scalar1=SRC_[:, sc:sc + 1], scalar2=0.0,
                                    op0=ALU.subtract, op1=ALU.is_equal)
                dvs[sc] = comp("dve", mk_p1s, waits=w)

                def mk_smm(t, sc=sc):
                    for nb in range(4):
                        t.matmul(gbank(4 + nb)[0:64, :],
                                 YTM[:, sc * 68:sc * 68 + 64],
                                 P1S[sc % 2][:, nb * 512:(nb + 1) * 512],
                                 start=(sc == 0), stop=(sc == 63))
                mms[sc] = comp("pe", mk_smm, waits=[dvs[sc]])

            if sub == 0:
                def mk_fev(v, fc=fc, r=r):
                    for nb in range(4):
                        v.tensor_copy(
                            out=AO[r:r + 64, fc * T + nb * 512: fc * T + (nb + 1) * 512],
                            in_=gbank(4 + nb)[0:64, :])
                fev = comp("dve", mk_fev, waits=[mms[63]])
            else:
                def mk_fev(v):
                    for nb in range(4):
                        v.tensor_copy(out=SCR64[:, nb * 512:(nb + 1) * 512],
                                      in_=gbank(4 + nb)[0:64, :])
                fev = comp("dve", mk_fev, waits=[mms[63]])
                dma("gp", AO[64:128, fc * T:(fc + 1) * T], SCR64[:], waits=[fev])

        for b in range(2):
            if stages <= 0:
                for ti in range(NTILE):
                    w = [("sd", sch.now("sd"))]
                    ltg = dma("gp", t32a[:, 0:E], xin[b, ti * 128:(ti + 1) * 128, :], waits=w)
                    dma("sync", xout[b, ti * 128:(ti + 1) * 128, :], t32a[:, 0:E], waits=[ltg])
                continue
            xld = []
            for ti in range(NTILE):
                w = [("dve", sch.now("dve")), ("sd", sch.now("sd"))] if ti == 0 else []
                xld.append(dma("gp", A1[:, ti * E:(ti + 1) * E],
                               xin[b, ti * 128:(ti + 1) * 128, :], waits=w))
            if stages == 100:
                for ti in range(NTILE):
                    w = [("gd", sch.now("gd"))] if ti == 0 else [("sd", sch.now("sd"))]
                    def mk_c(v, ti=ti):
                        v.tensor_copy(out=t32a[:, 0:E], in_=A1[:, ti * E:(ti + 1) * E])
                    ftg = comp("dve", mk_c, waits=w)
                    dma("sync", xout[b, ti * 128:(ti + 1) * 128, :], t32a[:, 0:E], waits=[ftg])
                continue
            lastz = ln_tm(A1, A2, waits=("gd",))
            if stages > 1:
                tm_to_fm(A2, Z0, waits=[lastz])

            if stages <= 1:
                for ti in range(NTILE):
                    w = [("dve", sch.now("dve")), ("pe", sch.now("pe"))] if ti == 0 else []
                    def mk_c(v, ti=ti):
                        v.tensor_copy(out=t32a[:, 0:E], in_=A2[:, ti * E:(ti + 1) * E])
                    ftg = comp("dve", mk_c, waits=w + ([("sd", sch.now("sd"))] if ti > 0 else []))
                    dma("sync", xout[b, ti * 128:(ti + 1) * 128, :], t32a[:, 0:E], waits=[ftg])
                continue

            def mk_zero(v):
                v.memset(AO[:], 0.0)
            aoz = comp("dve", mk_zero, waits=())

            for g in range(8):
                lq = dma("gp", WS[0][:, :], w_qk[g], waits=[("pe", sch.now("pe"))])
                lv = dma("gp", WS[1][:, :], w_v[g], waits=[("pe", sch.now("pe"))])
                prev_ev = {}
                for which, (wld, dst) in enumerate([(lq, QKG), (lv, VG)]):
                    for tsl in range(4):
                        bk = 2 + (which * 4 + tsl) % 2
                        w = [wld]
                        if (which * 4 + tsl) >= 2:
                            w.append(prev_ev[(which * 4 + tsl) % 2])
                        if g == 0 and which == 0 and tsl == 0:
                            w.append(("dve", sch.now("dve")))

                        def mk_mm(t, which=which, tsl=tsl, bk=bk):
                            wb = WS[which]
                            for ic in range(8):
                                t.matmul(pbank(bk)[:],
                                         wb[:, ic * 128:(ic + 1) * 128],
                                         Z0[:, ic * T + tsl * 512: ic * T + (tsl + 1) * 512],
                                         start=(ic == 0), stop=(ic == 7))
                        mtg = comp("pe", mk_mm, waits=w)

                        def mk_ev(v, tsl=tsl, bk=bk, dst=dst):
                            v.tensor_copy(out=dst[:, tsl * 512:(tsl + 1) * 512],
                                          in_=pbank(bk)[:])
                        prev_ev[(which * 4 + tsl) % 2] = comp("dve", mk_ev, waits=[mtg])
                qkv_done = [prev_ev[0], prev_ev[1]]

                for sub in range(2):
                    hd = g * 2 + sub
                    if hd < NLOC:
                        local_head(b, g, sub, qkv_done)
                    else:
                        lsh_head(b, g, sub, qkv_done)

            def sa_ev(v, bank, oc, tsl):
                v.scalar_tensor_tensor(
                    out=A1[:, oc * T + tsl * 512: oc * T + (tsl + 1) * 512],
                    in0=bank[:], scalar=s_pc[:, oc:oc + 1],
                    in1=Z0[:, oc * T + tsl * 512: oc * T + (tsl + 1) * 512],
                    op0=ALU.add, op1=ALU.add)
            proj_fm(w_out, 8, 8, AO, sa_ev, waits=[aoz, ("dve", sch.now("dve"))])

            if stages <= 2:
                for ti in range(NTILE):
                    w = [("dve", sch.now("dve"))] if ti == 0 else [("sd", sch.now("sd"))]
                    def mk_c(v, ti=ti):
                        for fc in range(FC):
                            v.tensor_copy(out=t32a[:, fc * 128:(fc + 1) * 128],
                                          in_=A1[:, fc * T + ti * 128: fc * T + (ti + 1) * 128])
                    ftg = comp("dve", mk_c, waits=w)
                    dma("sync", xout[b, ti * 128:(ti + 1) * 128, :], t32a[:, 0:E], waits=[ftg])
                continue

            fm_to_tm(A1, A2, waits=[("dve", sch.now("dve"))])
            l1 = ln_tm(A2, A2, waits=[("dve", sch.now("dve"))])
            tm_to_fm(A2, A1, waits=[l1],
                     gcol=lambda fc: s_pc[:, 8 + fc:9 + fc],
                     bcol=lambda fc: s_pc[:, 16 + fc:17 + fc])
            y1_done = ("dve", sch.now("dve"))
            if stages <= 3:
                for ti in range(NTILE):
                    w = [y1_done] if ti == 0 else [("sd", sch.now("sd"))]
                    def mk_c(v, ti=ti):
                        for fc in range(FC):
                            v.tensor_copy(out=t32a[:, fc * 128:(fc + 1) * 128],
                                          in_=A1[:, fc * T + ti * 128: fc * T + (ti + 1) * 128])
                    ftg = comp("dve", mk_c, waits=w)
                    dma("sync", xout[b, ti * 128:(ti + 1) * 128, :], t32a[:, 0:E], waits=[ftg])
                continue

            harena = {0: A3, 1: AO}
            prev_ff2_ev = None
            for half in range(2):
                ld = {}
                w0 = [("pe", sch.now("pe"))]
                if prev_ff2_ev is not None:
                    w0.append(prev_ff2_ev)
                ld[0] = dma("gp", WS[0][:, :], w_ff1[0], waits=w0)
                evh = [None, None]
                mmh = {}
                for oc in range(32):
                    if oc + 1 < 32:
                        w = [mmh[(oc - 1, 1)]] if oc >= 1 else []
                        ld[oc + 1] = dma("gp", WS[(oc + 1) % 2][:, :],
                                         w_ff1[oc + 1], waits=w)
                    for sl in range(2):
                        tsl = half * 2 + sl
                        bk = 2 + (oc * 2 + sl) % 2
                        w = [ld[oc]]
                        if oc == 0 and sl == 0:
                            w.append(y1_done)
                            if prev_ff2_ev is not None:
                                w.append(prev_ff2_ev)
                        if evh[(oc * 2 + sl) % 2] is not None:
                            w.append(evh[(oc * 2 + sl) % 2])

                        def mk_mm(t, oc=oc, sl=sl, tsl=tsl, bk=bk):
                            wb = WS[oc % 2]
                            for ic in range(8):
                                t.matmul(pbank(bk)[:],
                                         wb[:, ic * 128:(ic + 1) * 128],
                                         A1[:, ic * T + tsl * 512: ic * T + (tsl + 1) * 512],
                                         start=(ic == 0), stop=(ic == 7))
                        mmh[(oc, sl)] = comp("pe", mk_mm, waits=w)

                        def mk_ev(a, oc=oc, sl=sl, bk=bk):
                            ar = harena[oc // 16]
                            oo = (oc % 16) * 1024 + sl * 512
                            a.activation(out=ar[:, oo:oo + 512], in_=pbank(bk)[:],
                                         func=AF.Gelu,
                                         bias=s_bff1[:, oc:oc + 1], scale=1.0)
                        evh[(oc * 2 + sl) % 2] = comp(
                            "act", mk_ev, waits=[mmh[(oc, sl)]])
                h_done = ("act", sch.now("act"))

                for sl in range(2):
                    tsl = half * 2 + sl
                    ld2 = {}
                    ld2[0] = dma("gp", WS[2][:, :], w_ff2[0:128, :],
                                 waits=[("pe", sch.now("pe"))])
                    mmlast = None
                    for oc in range(32):
                        if oc + 1 < 32:
                            w = [mmlast] if oc >= 1 else []
                            ld2[oc + 1] = dma("gp", WS[2 + (oc + 1) % 2][:, :],
                                              w_ff2[(oc + 1) * 128:(oc + 2) * 128, :],
                                              waits=w)
                        w = [ld2[oc]]
                        if oc == 0:
                            w.append(h_done)
                            if prev_ff2_ev is not None:
                                w.append(prev_ff2_ev)

                        def mk_mm(t, oc=oc, sl=sl):
                            wb = WS[2 + oc % 2]
                            ar = harena[oc // 16]
                            oo = (oc % 16) * 1024 + sl * 512
                            for f2 in range(8):
                                t.matmul(pbank(f2)[:],
                                         wb[:, f2 * 128:(f2 + 1) * 128],
                                         ar[:, oo:oo + 512],
                                         start=(oc == 0), stop=(oc == 31))
                        mmlast = comp("pe", mk_mm, waits=w)

                    def mk_ev(v, tsl=tsl):
                        for f2 in range(8):
                            v.scalar_tensor_tensor(
                                out=A2[:, f2 * T + tsl * 512: f2 * T + (tsl + 1) * 512],
                                in0=pbank(f2)[:], scalar=s_pc[:, 56 + f2:57 + f2],
                                in1=A1[:, f2 * T + tsl * 512: f2 * T + (tsl + 1) * 512],
                                op0=ALU.add, op1=ALU.add)
                    prev_ff2_ev = comp("dve", mk_ev, waits=[mmlast])

            fm_to_tm(A2, A3, waits=[("dve", sch.now("dve")), ("act", sch.now("act"))])
            l2 = ln_tm(A3, A3, waits=[("dve", sch.now("dve"))])
            tm_to_fm(A3, A2, waits=[l2],
                     gcol=lambda fc: s_pc[:, 24 + fc:25 + fc],
                     bcol=lambda fc: s_pc[:, 32 + fc:33 + fc])
            fm_to_tm(A2, A3, waits=[("dve", sch.now("dve"))])
            l3 = ln_tm(A3, A3, waits=[("dve", sch.now("dve"))])
            xld2 = []
            for ti in range(NTILE):
                xld2.append(dma("gp", A2[:, ti * E:(ti + 1) * E],
                                xin[b, ti * 128:(ti + 1) * 128, :],
                                waits=([l3] if ti == 0 else [])))

            def mk_x1(v):
                for ti in range(NTILE):
                    v.tensor_tensor(out=A1[:, ti * E:(ti + 1) * E],
                                    in0=A2[:, ti * E:(ti + 1) * E],
                                    in1=A3[:, ti * E:(ti + 1) * E], op=ALU.add)
            x1t = comp("dve", mk_x1, waits=[l3, ("gd", sch.now("gd"))])

            if stages <= 4:
                for ti in range(NTILE):
                    w = [x1t] if ti == 0 else [("sd", sch.now("sd"))]
                    def mk_c(v, ti=ti):
                        v.tensor_copy(out=t32a[:, 0:E], in_=A1[:, ti * E:(ti + 1) * E])
                    ftg = comp("dve", mk_c, waits=w)
                    dma("sync", xout[b, ti * 128:(ti + 1) * 128, :], t32a[:, 0:E], waits=[ftg])
                continue

            lz = ln_tm(A1, A2, waits=[x1t])
            tm_to_fm(A2, A3, waits=[lz])
            zfm_done = ("dve", sch.now("dve"))

            def mk_mix2(v):
                for fc in range(FC):
                    zf = A3[:, fc * T:(fc + 1) * T]
                    xr = A2[:, fc * T:(fc + 1) * T]
                    xk = AO[:, fc * T:(fc + 1) * T]
                    mur = s_pc[:, 40 + fc:41 + fc]
                    muk = s_pc[:, 48 + fc:49 + fc]
                    v.tensor_scalar(out=xr[:, 0:1], in0=zf[:, 0:1], scalar1=mur,
                                    scalar2=None, op0=ALU.mult)
                    v.tensor_scalar(out=xk[:, 0:1], in0=zf[:, 0:1], scalar1=muk,
                                    scalar2=None, op0=ALU.mult)
                    for cchunk in range(2):
                        lo = 1 + cchunk * 1023
                        hi = lo + 1023 + (1 if cchunk == 1 else 0)
                        n = hi - lo
                        v.tensor_tensor(out=t32a[:, 0:n], in0=zf[:, lo:lo + n],
                                        in1=zf[:, lo - 1:lo - 1 + n], op=ALU.subtract)
                        v.scalar_tensor_tensor(
                            out=xr[:, lo:lo + n], in0=t32a[:, 0:n], scalar=mur,
                            in1=zf[:, lo - 1:lo - 1 + n], op0=ALU.mult, op1=ALU.add)
                        v.scalar_tensor_tensor(
                            out=xk[:, lo:lo + n], in0=t32a[:, 0:n], scalar=muk,
                            in1=zf[:, lo - 1:lo - 1 + n], op0=ALU.mult, op1=ALU.add)
            mixdone = comp("dve", mk_mix2, waits=[zfm_done])

            def kact_ev(v, bank, oc, tsl):
                v.tensor_scalar(out=t32a[:, 0:512], in0=bank[:], scalar1=0.0,
                                scalar2=None, op0=ALU.max)
                v.drain()
                v.tensor_tensor(
                    out=A3[:, oc * T + tsl * 512: oc * T + (tsl + 1) * 512],
                    in0=t32a[:, 0:512], in1=t32a[:, 0:512], op=ALU.mult)
            proj_fm(w_k, 8, 8, AO, kact_ev, waits=[mixdone], wbufs=(0, 1),
                    pbanks=(2, 3))
            kact_done = ("dve", sch.now("dve"))

            ldr = {}
            ldv = {}
            ldr[0] = dma("gp", WS[0][:, :], w_r[0], waits=[kact_done])
            ldv[0] = dma("gp", WS[2][:, :], w_v2[0], waits=[kact_done])
            evr = [None, None]
            sig = [None, None]
            mmr = {}
            for oc in range(8):
                if oc + 1 < 8:
                    w = [mmr[(oc - 1, 3, "v")]] if oc >= 1 else []
                    ldr[oc + 1] = dma("gp", WS[(oc + 1) % 2][:, :], w_r[oc + 1], waits=w)
                    ldv[oc + 1] = dma("gp", WS[2 + (oc + 1) % 2][:, :], w_v2[oc + 1], waits=w)
                for tsl in range(4):
                    i2 = (oc * 4 + tsl) % 2
                    bkr = 4 + i2
                    bkv = 6 + i2
                    w = [ldr[oc], ldv[oc]]
                    if evr[i2] is not None:
                        w.append(evr[i2])

                    def mk_mm(t, oc=oc, tsl=tsl, bkr=bkr, bkv=bkv):
                        for ic in range(8):
                            t.matmul(pbank(bkr)[:],
                                     WS[oc % 2][:, ic * 128:(ic + 1) * 128],
                                     A2[:, ic * T + tsl * 512: ic * T + (tsl + 1) * 512],
                                     start=(ic == 0), stop=(ic == 7))
                        for ic in range(8):
                            t.matmul(pbank(bkv)[:],
                                     WS[2 + oc % 2][:, ic * 128:(ic + 1) * 128],
                                     A3[:, ic * T + tsl * 512: ic * T + (tsl + 1) * 512],
                                     start=(ic == 0), stop=(ic == 7))
                    mmr[(oc, tsl, "v")] = comp("pe", mk_mm, waits=w)

                    def mk_sig(a, oc=oc, tsl=tsl, bkr=bkr, i2=i2):
                        a.activation(out=t32a[:, i2 * 512:i2 * 512 + 512],
                                     in_=pbank(bkr)[:], func=AF.Sigmoid)
                    sig[i2] = comp("act", mk_sig, waits=[mmr[(oc, tsl, "v")]])

                    def mk_out(v, oc=oc, tsl=tsl, bkv=bkv, i2=i2):
                        v.tensor_tensor(
                            out=AO[:, oc * T + tsl * 512: oc * T + (tsl + 1) * 512],
                            in0=t32a[:, i2 * 512:i2 * 512 + 512],
                            in1=pbank(bkv)[:], op=ALU.mult)
                    evr[i2] = comp("dve", mk_out, waits=[sig[i2]])

            fm_to_tm(AO, A2, waits=[("dve", sch.now("dve")), ("act", sch.now("act"))])
            lr = ln_tm(A2, A2, waits=[("dve", sch.now("dve"))])

            o32b = A3[:].bitcast(F32)
            sttile = {}
            for ti in range(NTILE):
                w = [lr] if ti == 0 else []
                if ti >= 2:
                    w.append(sttile[ti - 2])
                off = (ti % 2) * E

                def mk_fin(v, ti=ti, off=off):
                    v.tensor_tensor(out=o32b[:, off:off + E],
                                    in0=A1[:, ti * E:(ti + 1) * E],
                                    in1=A2[:, ti * E:(ti + 1) * E], op=ALU.add)
                ftg = comp("dve", mk_fin, waits=w)
                sttile[ti] = dma("sync", xout[b, ti * 128:(ti + 1) * 128, :],
                                 o32b[:, off:off + E], waits=[ftg])

        def run_engine(ename):
            def go(e):
                for w, fn, tgt in sch.ops[ename]:
                    fn(e, w)
            return go

        block.tensor(run_engine("pe"))
        block.vector(run_engine("dve"))
        block.scalar(run_engine("act"))
        block.sync(run_engine("sync"))
        block.gpsimd(run_engine("gp"))

        def fin(e):
            e.wait_ge(sems["sd"], sch.c["sd"])
            e.wait_ge(sems["gd"], sch.c["gd"])
        block.sync(fin)

    return nc


def _pack_oc(wT, n_oc):
    K, M = wT.shape
    nic = K // 128
    w4 = wT.reshape(nic, 128, n_oc, 128)
    return np.ascontiguousarray(w4.transpose(2, 1, 0, 3).reshape(n_oc, 128, K))


def _aux_inputs():
    aux = {}
    i64 = np.arange(64, dtype=np.float32)
    i128 = np.arange(128, dtype=np.float32)
    i2048 = np.arange(2048, dtype=np.float32)
    aux["idn_bf"] = np.eye(128, dtype=np.float32)
    aux["idn_f16"] = np.eye(128, dtype=np.float32)
    aux["idn_f32"] = np.eye(128, dtype=np.float32)
    aux["supper_bf"] = (i128[:, None] < i128[None, :]).astype(np.float32)
    v160 = np.arange(160, dtype=np.float32)
    sb5 = np.zeros((32, 800), np.float32)
    for bb in range(5):
        vp = 32 * bb + np.arange(32)
        sb5[:, bb * 160:(bb + 1) * 160] = (vp[:, None] < v160[None, :])
    aux["slt_b5"] = sb5
    aux["iota64r"] = np.broadcast_to(i64, (128, 64)).copy()
    aux["iota128r"] = np.broadcast_to(i128, (128, 128)).copy()
    aux["iota2048r"] = np.broadcast_to(i2048, (128, 2048)).copy()
    aux["iotacol"] = i128[:, None].copy()
    aux["thresh64"] = np.broadcast_to(128.0 * i64, (128, 64)).copy()
    aux["ones_f32"] = np.ones((128, 128), np.float32)
    aux["ones_bf"] = np.ones((128, 128), np.float32)
    aux["ones_f16"] = np.ones((128, 128), np.float32)
    d = (np.arange(256)[None, :] - 128 - np.arange(128)[:, None])
    m = np.where(d > 0, MVAL, np.where(d == 0, TSV, 0.0)).astype(np.float32)
    aux["mask_w"] = m
    hj = np.arange(256)
    aux["ju"] = np.broadcast_to(((hj % 64) + 32 * (hj // 64)).astype(np.float32),
                                (128, 256)).copy()
    mh = np.zeros((128, 192), np.float32)
    mh[0:64, 128:192] = MVAL
    mh[64:128, 0:64] = MVAL
    aux["mask_half"] = mh
    aux["ival"] = (np.arange(128)[:, None] + 128.0 * np.arange(16)[None, :]
                   ).astype(np.float32)
    aux["ival16"] = aux["ival"].copy()
    cc = np.zeros((128, 4), np.float32)
    cc[:, 0] = LNE; cc[:, 1] = EPSN; cc[:, 2] = 1.0
    aux["constc"] = cc
    return aux


_BUILT = {}


def kernel(L0_attn_x, L0_attn_w_qk, L0_attn_w_v, L0_attn_w_out, L0_attn_b_out,
           L0_attn_w_ff1, L0_attn_b_ff1, L0_attn_w_ff2, L0_attn_b_ff2,
           L0_attn_ln1_g, L0_attn_ln1_b, L0_attn_ln2_g, L0_attn_ln2_b,
           L0_attn_rotations, L0_ffn_W_r, L0_ffn_W_k, L0_ffn_W_v,
           L0_ffn_mu_r, L0_ffn_mu_k):
    import ml_dtypes
    f32 = np.float32

    def packcols(vecs):
        out = np.zeros((128, 64), f32)
        for j, v in enumerate(vecs):
            out[:, j * 8:(j + 1) * 8] = np.asarray(v, f32).reshape(8, 128).T
        return out

    pc = np.zeros((128, 64), f32)
    for j, v in enumerate([L0_attn_b_out, L0_attn_ln1_g, L0_attn_ln1_b,
                           L0_attn_ln2_g, L0_attn_ln2_b, L0_ffn_mu_r,
                           L0_ffn_mu_k, L0_attn_b_ff2]):
        pc[:, j * 8:(j + 1) * 8] = np.asarray(v, f32).reshape(8, 128).T
    bff1c = np.asarray(L0_attn_b_ff1, f32).reshape(32, 128).T.copy()

    common = {
        "w_qk": _pack_oc(np.ascontiguousarray(np.asarray(L0_attn_w_qk, f32).T), 8),
        "w_v": _pack_oc(np.ascontiguousarray(np.asarray(L0_attn_w_v, f32).T), 8),
        "w_out": _pack_oc(np.ascontiguousarray(np.asarray(L0_attn_w_out, f32).T), 8),
        "w_ff1": _pack_oc(np.ascontiguousarray(np.asarray(L0_attn_w_ff1, f32).T), 32),
        "w_ff2": np.ascontiguousarray(np.asarray(L0_attn_w_ff2, f32).T),
        "w_r": _pack_oc(np.ascontiguousarray(np.asarray(L0_ffn_W_r, f32).T), 8),
        "w_k": _pack_oc(np.ascontiguousarray(np.asarray(L0_ffn_W_k, f32).T), 8),
        "w_v2": _pack_oc(np.ascontiguousarray(np.asarray(L0_ffn_W_v, f32).T), 8),
        "rotc": np.ascontiguousarray(np.asarray(L0_attn_rotations, f32).reshape(DH, 128)),
        "pcols": pc, "bff1c": bff1c,
    }
    aux = _aux_inputs()
    for k in ("idn_bf", "supper_bf", "ones_bf", "iota64r", "iota128r",
              "thresh64", "ju"):
        aux[k] = aux[k].astype(ml_dtypes.bfloat16)
    for k in ("idn_f16", "ones_f16", "iota2048r", "ival16"):
        aux[k] = aux[k].astype(np.float16)
    common.update(aux)

    import os
    stages = int(os.environ.get("KSTAGES", "99"))
    klsh = int(os.environ.get("KLSH", "99"))
    kdump = os.environ.get("KDUMP", "")
    key = ("nc", stages, klsh, kdump)
    if key not in _BUILT:
        _BUILT[key] = build_nc(stages, klsh, kdump)
    nc = _BUILT[key]

    x = np.asarray(L0_attn_x, f32)
    ncores = int(os.environ.get("KCORES", "8"))
    in_maps = []
    for core in range(ncores):
        m = dict(common)
        m["xin"] = np.ascontiguousarray(x[2 * core:2 * core + 2])
        in_maps.append(m)
    res = run_bass_kernel_spmd(nc, in_maps, core_ids=list(range(ncores)))
    if os.environ.get("KDUMP"):
        np.save("/root/problem/_dbg.npy", np.asarray(res.results[0]["dbg"]))
    out = np.concatenate([np.asarray(r["xout"]) for r in res.results], axis=0)
    if ncores < 8:
        out = np.concatenate([out] + [np.zeros_like(out[:2])] * (8 - ncores), axis=0)
    return out.reshape(16, T, E).astype(np.float32)
